# version 05
# baseline (speedup 1.0000x reference)
import contextlib
from collections import defaultdict

import numpy as np
import concourse.bass as bass
import concourse.mybir as mybir
from concourse.bass_utils import run_bass_kernel_spmd

F32 = mybir.dt.float32
BF16 = mybir.dt.bfloat16
AF = mybir.ActivationFunctionType
ALU = mybir.AluOpType
AX = mybir.AxisListType

CH = 512
NKT = 8
SBUF_LIMIT = 212863

ENGINES = ("tensor", "scalar", "vector", "gpsimd", "sync")


class Emitter:
    def __init__(self):
        self.q = {e: [] for e in ENGINES}
        self.cnt = defaultdict(int)

    def op(self, eng, fn, waits=(), inc=None, inc_n=1):
        waits = tuple((s, v) for (s, v) in waits if v > 0)
        if inc is not None:
            self.cnt[inc] += inc_n
        self.q[eng].append((waits, fn, inc, inc_n))
        return self.cnt[inc] if inc else None

    def v(self, name):
        return self.cnt[name]

    def play(self, eng, engobj, sems):
        for waits, fn, inc, inc_n in self.q[eng]:
            for s, val in waits:
                engobj.wait_ge(sems[s], val)
            if fn is None:
                continue
            inst = fn(engobj)
            if inc is not None:
                inst.then_inc(sems[inc], inc_n)

    def n_ops(self):
        return {e: len(v) for e, v in self.q.items()}


def build_program(n_chunks=4, with_attn=True, debug_dump=False):
    nc = bass.Bass("TRN2", target_bir_lowering=False, debug=False)
    T = n_chunks * CH

    dx = nc.dram_tensor("x", [T, 1024], F32, kind="ExternalInput").ap()
    dwq = nc.dram_tensor("wq", [1024, 1024], F32, kind="ExternalInput").ap()
    dwk = nc.dram_tensor("wk", [1024, 1024], F32, kind="ExternalInput").ap()
    dwv = nc.dram_tensor("wv", [1024, 1024], F32, kind="ExternalInput").ap()
    dwo = nc.dram_tensor("wo", [1024, 1024], F32, kind="ExternalInput").ap()
    dw1 = nc.dram_tensor("w1", [1024, 4096], F32, kind="ExternalInput").ap()
    dw2 = nc.dram_tensor("w2", [4096, 1024], F32, kind="ExternalInput").ap()
    dbq = nc.dram_tensor("bq", [128, 8], F32, kind="ExternalInput").ap()
    dbk = nc.dram_tensor("bk", [128, 8], F32, kind="ExternalInput").ap()
    dbv = nc.dram_tensor("bv", [128, 8], F32, kind="ExternalInput").ap()
    dbo = nc.dram_tensor("bo", [128, 8], F32, kind="ExternalInput").ap()
    db2 = nc.dram_tensor("b2", [128, 8], F32, kind="ExternalInput").ap()
    db1 = nc.dram_tensor("b1", [1, 4096], F32, kind="ExternalInput").ap()
    did_f = nc.dram_tensor("id_f", [128, 128], F32, kind="ExternalInput").ap()
    did_b = nc.dram_tensor("id_b", [128, 128], BF16, kind="ExternalInput").ap()
    dones = nc.dram_tensor("ones_b", [128, 128], BF16, kind="ExternalInput").ap()
    dones_f = nc.dram_tensor("ones_f", [128, 128], F32, kind="ExternalInput").ap()
    dout = nc.dram_tensor("out", [T, 1024], F32, kind="ExternalOutput").ap()
    scr = {
        "wq": nc.dram_tensor("scr_wq", [1024, 1024], BF16, kind="Internal").ap(),
        "wk": nc.dram_tensor("scr_wk", [1024, 1024], BF16, kind="Internal").ap(),
        "wv": nc.dram_tensor("scr_wv", [1024, 1024], BF16, kind="Internal").ap(),
        "wo": nc.dram_tensor("scr_wo", [1024, 1024], BF16, kind="Internal").ap(),
        "w1": nc.dram_tensor("scr_w1", [1024, 4096], BF16, kind="Internal").ap(),
        "w2": nc.dram_tensor("scr_w2", [4096, 1024], BF16, kind="Internal").ap(),
    }
    srcw = {"wq": dwq, "wk": dwk, "wv": dwv, "wo": dwo, "w1": dw1, "w2": dw2}
    scr_dummy = nc.dram_tensor("scr_dummy", [128, 1024], BF16,
                               kind="Internal").ap()

    off = [nc.sbuf_base]

    def at(name, shape, dtype, offset=None):
        o = off[0] if offset is None else offset
        o = (o + 31) // 32 * 32
        h = nc.alloc_sbuf_tensor_at(name, shape, dtype, offset=o)
        if offset is None:
            off[0] = o + shape[1] * mybir.dt.size(dtype)
        return h.ap(), o

    id_f, _ = at("id_f", [128, 128], F32)
    id_b, _ = at("id_b", [128, 128], BF16)
    ones_b, _ = at("ones_b", [128, 128], BF16)
    ones_f, _ = at("ones_f", [128, 128], F32)
    bqc, _ = at("bqc", [128, 8], F32)
    bkc, _ = at("bkc", [128, 8], F32)
    bvc, _ = at("bvc", [128, 8], F32)
    boc, _ = at("boc", [128, 8], F32)
    b2c, _ = at("b2c", [128, 8], F32)
    invz_bf, _ = at("invz_bf", [128, 8], BF16)
    zc_f, _ = at("zc_f", [128, 8], F32)
    eps_c, _ = at("eps_c", [128, 1], F32)
    fence_sb, _ = at("fence_sb", [128, 1], F32)
    afence_sb, _ = at("afence_sb", [128, 1], F32)
    nhalf_c, _ = at("nhalf_c", [128, 1], F32)
    _, _ = at("_pad", [128, 24], F32)
    b1row, _ = at("b1row", [1, 4096], BF16)
    mean_row, _ = at("mean_row", [1, 512], F32)
    rstd_row, _ = at("rstd_row", [1, 512], F32)
    zscr, zscr_o = at("zscr", [128, 4096], BF16)
    xpre_bf, _ = at("xpre_bf", [128, 4096], BF16, offset=zscr_o)
    w_sb, _ = at("w_sb", [128, 2048], BF16)
    xtm, _ = at("xtm", [128, 4096], F32)
    xt_bf, xt_o = at("xt_bf", [128, 4096], BF16)
    sq_bf, _ = at("sq_bf", [128, 4096], BF16, offset=xt_o)
    qt_bf, qt_o = at("qt_bf", [128, 4096], BF16)
    attn_bf, _ = at("attn_bf", [128, 4096], BF16, offset=qt_o)
    qs_bf, _ = at("qs_bf", [128, 4096], BF16)
    kt_bf, _ = at("kt_bf", [128, 4096], BF16)
    vt_bf, _ = at("vt_bf", [128, 4096], BF16)
    e_o = off[0]
    eS = {}
    eS[(0, 0)], _ = at("e00", [128, 8192], BF16)
    eS[(0, 1)], _ = at("e01", [128, 8192], BF16)
    eS[(1, 0)], _ = at("e10", [128, 8192], BF16)
    eS[(1, 1)], _ = at("e11", [128, 8192], BF16)
    htm = []
    for t in range(4):
        a, _ = at(f"htm{t}", [128, 4096], BF16, offset=e_o + t * 8192)
        htm.append(a)
    ht_bf, _ = at("ht_bf", [128, 16384], BF16, offset=e_o + 32768)
    x1f, _ = at("x1f", [128, 4096], F32)
    x2f, _ = at("x2f", [128, 4096], F32)
    x1_bf, _ = at("x1_bf", [128, 4096], BF16)
    wstg = []
    wstg_offs = []
    for i in range(2):
        a_, o_ = at(f"wstg{i}", [128, 1024], F32)
        wstg.append(a_)
        wstg_offs.append(o_)
    cbst = [at(f"cbst{i}", [128, 1024], BF16)[0] for i in range(2)]
    wbf = [at(f"wbf{i}", [128, 1024], BF16)[0] for i in range(2)]
    outtm = [at(f"outtm{i}", [128, 1024], F32, offset=wstg_offs[i])[0]
             for i in range(2)]
    assert off[0] <= nc.sbuf_top, f"SBUF overflow: {off[0]} > {nc.sbuf_top}"

    P = [nc.alloc_psum_tensor(f"ps{i}", [128, 1024], F32) for i in range(4)]

    def pv(i):
        return P[i // 2].ap()[:, (i % 2) * 512:(i % 2) * 512 + 512]

    T0, T1 = P[0].ap(), P[1].ap()
    PW = [pv(4), pv(5), pv(6), pv(7)]

    em = Emitter()
    st = {}
    pv_war = {i: [] for i in range(8)}

    def mm_first_waits(views):
        w = []
        for i in views:
            w.extend(pv_war[i])
        return w

    def read_view(i, sem):
        pv_war[i] = [(sem, em.v(sem))]

    def vfence():
        em.op("vector", lambda v: v.tensor_copy(fence_sb[:], fence_sb[:]),
              inc="dve")
        em.op("vector", lambda v: v.tensor_copy(fence_sb[:], fence_sb[:]),
              inc="dve")
        return em.v("dve")

    def afence():
        em.op("scalar", lambda sc_: sc_.copy(afence_sb[0:1, 0:1],
                                             afence_sb[0:1, 0:1]), inc="act")
        em.op("scalar", lambda sc_: sc_.copy(afence_sb[0:1, 0:1],
                                             afence_sb[0:1, 0:1]), inc="act")
        return em.v("act")

    def ldconst():
        em.op("vector", lambda v: v.memset(fence_sb[:], 0.0), inc="dve")
        em.op("vector", lambda v: v.memset(afence_sb[:], 0.0), inc="dve")
        em.op("vector", lambda v: v.memset(eps_c[:], 1e-5), inc="dve")
        em.op("vector", lambda v: v.memset(nhalf_c[:], -0.5), inc="dve")
        for dst, src in ((id_f, did_f), (id_b, did_b), (ones_b, dones),
                         (ones_f, dones_f),
                         (bqc, dbq), (bkc, dbk), (bvc, dbv), (boc, dbo),
                         (b2c, db2)):
            em.op("sync", lambda s, d=dst, x=src: s.dma_start(out=d, in_=x),
                  inc="dW", inc_n=16)
        st["consts_loaded"] = em.v("dW")
        for qd in range(4):
            em.op("sync", lambda s, q=qd: s.dma_start(
                out=wstg[0][0:1, 0:1024],
                in_=db1[0:1, q * 1024:(q + 1) * 1024]),
                waits=[("dve", em.v("dve"))], inc="dW", inc_n=16)
            em.op("vector", lambda v, q=qd: v.tensor_copy(
                b1row[0:1, q * 1024:(q + 1) * 1024], wstg[0][0:1, 0:1024]),
                waits=[("dW", em.v("dW"))], inc="dve")
        vfence()
        st[("wstg_cast", 0)] = em.v("dve")

    def weight_prepass():
        order = [("wq", [(k, 0, 1024) for k in range(8)]),
                 ("wk", [(k, 0, 1024) for k in range(8)]),
                 ("wv", [(k, 0, 1024) for k in range(8)]),
                 ("wo", [(k, 0, 1024) for k in range(8)]),
                 ("w1", [(k, iq * 1024, 1024) for iq in range(4)
                         for k in range(8)]),
                 ("w2", [(k, 0, 1024) for k in range(32)])]
        buf = 0
        for wid, blocks in order:
            for (k, c0, n) in blocks:
                sl = srcw[wid][k * 128:(k + 1) * 128, c0:c0 + n]
                scr_sl = scr[wid][k * 128:(k + 1) * 128, c0:c0 + n]
                em.op("sync", lambda s, a=sl, nn=n, b=buf: s.dma_start(
                    out=wstg[b][:, 0:nn], in_=a),
                    waits=[("dve", st.get(("wstg_cast", buf), 0))],
                    inc="dW", inc_n=16)
                dd = em.v("dW")
                em.op("vector", lambda v, b=buf, nn=n: v.tensor_copy(
                    cbst[b][:, 0:nn], wstg[b][:, 0:nn]),
                    waits=[("dW", dd), ("dS", st.get(("cbst_war", buf), 0))],
                    inc="dve")
                st[("wstg_cast", buf)] = em.v("dve")
                f = vfence()
                em.op("gpsimd", lambda g, a=scr_sl, b=buf, nn=n: g.dma_start(
                    out=a, in_=cbst[b][:, 0:nn]),
                    waits=[("dve", f)], inc="dS", inc_n=16)
                st[("cbst_war", buf)] = em.v("dS")
                buf ^= 1
            st[("wdone", wid)] = em.v("dS")
        for rep in range(2):
            em.op("gpsimd", lambda g: g.dma_start(
                out=scr_dummy[0:128, 0:1024], in_=cbst[0][:, 0:1024]),
                waits=[("dve", st[("wstg_cast", 0)])], inc="dS", inc_n=16)
        st[("wdone", "w2")] = em.v("dS")
        nxt = {"wq": "wk", "wk": "wv", "wv": "wo", "wo": "w1",
               "w1": "w2", "w2": "w2"}
        for wid in ("wq", "wk", "wv", "wo", "w1", "w2"):
            st[("wsettle", wid)] = st[("wdone", nxt[wid])]

    def w_kblock(wid, k, buf, cols=None, chunk=0):
        c0 = cols[0] if cols is not None else 0
        n = (cols[1] - cols[0]) if cols is not None else srcw[wid].shape[1]
        scr_sl = scr[wid][k * 128:(k + 1) * 128, c0:c0 + n]
        em.op("sync", lambda s, a=scr_sl, b=buf, nn=n: s.dma_start(
            out=wbf[b][:, 0:nn], in_=a),
            waits=[("dS", st[("wsettle", wid)]),
                   ("pe", st.get(("wbf_war", buf), 0))],
            inc="dW", inc_n=16)
        st[("wbf_ready", buf)] = ("dW", em.v("dW"))

    def phase_a(c):
        for tt in range(4):
            em.op("gpsimd", lambda g, t=tt, cc=c: g.dma_start(
                out=xtm[:, t * 1024:(t + 1) * 1024],
                in_=dx[cc * CH + t * 128: cc * CH + (t + 1) * 128, :]),
                waits=[("pe", st.get("xtm_free", 0))], inc="dX", inc_n=16)
        x_loaded = em.v("dX")
        for ft in range(NKT):
            for tt in range(4):
                em.op("tensor", lambda t, f=ft, q=tt: t.matmul(
                    pv(f)[:, q * 128:(q + 1) * 128],
                    xtm[:, q * 1024 + f * 128: q * 1024 + (f + 1) * 128],
                    id_f, is_transpose=True, start=True, stop=True),
                    waits=([("dX", x_loaded), ("dW", st["consts_loaded"])] +
                           (mm_first_waits([ft]) if tt == 0 else [])),
                    inc="pe" if tt == 3 else None)
            fill = em.v("pe")
            em.op("vector", lambda v, f=ft: v.tensor_copy(
                xt_bf[:, f * 512:(f + 1) * 512], pv(f)), waits=[("pe", fill)],
                inc="dve")
            vfence()
            read_view(ft, "dve")
        for (wd, dst, bias, scaled) in (("wq", qt_bf, bqc, True),
                                        ("wk", kt_bf, bkc, True),
                                        ("wv", vt_bf, bvc, False)):
            for k in range(NKT):
                w_kblock(wd, k, k % 2, chunk=c)
                wready = st[("wbf_ready", k % 2)]
                for m in range(NKT):
                    wl = [wready]
                    if k == 0:
                        wl += mm_first_waits([m])
                    em.op("tensor", lambda t, kk=k, mm_=m: t.matmul(
                        pv(mm_), wbf[kk % 2][:, mm_ * 128:(mm_ + 1) * 128],
                        xt_bf[:, kk * 512:(kk + 1) * 512],
                        start=(kk == 0), stop=(kk == NKT - 1)),
                        waits=wl, inc="pe" if m == NKT - 1 else None)
                st[("wbf_war", k % 2)] = em.v("pe")
            filled = em.v("pe")
            for m in range(NKT):
                if scaled:
                    em.op("vector", lambda v, mm_=m, d=dst, bb=bias: v.tensor_scalar(
                        d[:, mm_ * 512:(mm_ + 1) * 512], pv(mm_),
                        bb[:, mm_:mm_ + 1], 0.03125, ALU.add, op1=ALU.mult),
                        waits=[("pe", filled)], inc="dve")
                else:
                    em.op("vector", lambda v, mm_=m, d=dst, bb=bias: v.tensor_scalar(
                        d[:, mm_ * 512:(mm_ + 1) * 512], pv(mm_),
                        bb[:, mm_:mm_ + 1], None, ALU.add),
                        waits=[("pe", filled)], inc="dve")
                read_view(m, "dve")
        qdone = em.v("dve")
        em.op("vector", lambda v: v.tensor_copy(qs_bf[0:64, :], qt_bf[64:128, :]),
              waits=[("dve", qdone)], inc="dve")
        em.op("vector", lambda v: v.tensor_copy(qs_bf[64:128, :], qt_bf[0:64, :]),
              inc="dve")
        st["qkv_ready"] = vfence()

    def phase_b(c):
        NPAIR = 32
        st["invz_slot"] = st.get("invz_slot", {})

        def s_mm(t, row, u, mb, j, dstv):
            lhs_t = kt_bf[row:row + 64,
                          u * 512 + mb * 128: u * 512 + (mb + 1) * 128]
            natural = (j % 2 == 0) if row == 0 else (j % 2 == 1)
            src = qt_bf if natural else qs_bf
            rhs = src[row:row + 64, (j // 2) * 512:(j // 2) * 512 + 512]
            return t.matmul(dstv, lhs_t, rhs, start=True, stop=True,
                            tile_position=(row, 0))

        def w_group(t, par, which, g, first, last, slot):
            ebuf = eS[(par, which)]
            inst = None
            for r in range(4):
                j = 4 * r + g
                inst = t.matmul(PW[g][32 * r:32 * r + 1, :],
                                invz_bf[:, slot:slot + 1],
                                ebuf[:, j * 512:(j + 1) * 512],
                                start=first, stop=last,
                                tile_position=(0, 32 * r))
            return inst

        for pi in range(NPAIR):
            u, mb = pi % 8, pi // 8
            par = pi % 2
            for jj in range(NKT):
                jlo, jhi = 2 * jj, 2 * jj + 1
                wl0 = [("act", st.get("t0_drained", 0)),
                       ("dve", st.get("qkv_ready", 0))]
                wl1 = [("act", st.get("t1_drained", 0))]
                if pi == 0 and jj == 0:
                    wl0 += mm_first_waits([0, 1])
                    wl1 += mm_first_waits([2, 3])
                em.op("tensor", lambda t, uu=u, m=mb, j=jlo: s_mm(
                    t, 0, uu, m, j, T0[:, 0:512]), waits=wl0)
                em.op("tensor", lambda t, uu=u, m=mb, j=jlo: s_mm(
                    t, 64, uu, m, j, T1[:, 0:512]), waits=wl1)
                em.op("tensor", lambda t, uu=u, m=mb, j=jhi: s_mm(
                    t, 0, uu, m, j, T0[:, 512:1024]), inc="pe")
                t0f = em.v("pe")
                em.op("tensor", lambda t, uu=u, m=mb, j=jhi: s_mm(
                    t, 64, uu, m, j, T1[:, 512:1024]), inc="pe")
                t1f = em.v("pe")
                wlA = [("pe", t0f)]
                if jj == 0:
                    wlA.append(("pe", st.get(("e_war", par), 0)))
                em.op("scalar", lambda s, e=eS[(par, 0)], k=jj: s.activation(
                    e[:, k * 1024:(k + 1) * 1024], T0[:, 0:1024], AF.Exp),
                    waits=wlA, inc="act")
                st["t0_drained"] = em.v("act")
                em.op("scalar", lambda s, e=eS[(par, 1)], k=jj: s.activation(
                    e[:, k * 1024:(k + 1) * 1024], T1[:, 0:1024], AF.Exp),
                    waits=[("pe", t1f)], inc="act")
                st["t1_drained"] = em.v("act")
                if jj == NKT - 1:
                    st[("e_done", par, 0)] = st["t0_drained"]
                    st[("e_done", par, 0)] = st["t0_drained"] + 1
                    if pi == NPAIR - 1:
                        st[("e_done", par, 1)] = afence()
                    else:
                        st[("e_done", par, 1)] = st["t1_drained"] + 1
                if pi > 0 and jj >= 4:
                    ppar = (pi - 1) % 2
                    which = 0 if jj < 6 else 1
                    first = (pi - 1 == 0) and which == 0
                    last = (pi - 1 == NPAIR - 1)
                    for gi, g in enumerate((2 * (jj % 2), 2 * (jj % 2) + 1)):
                        wl = [("dve", st[("invz_ready", ppar, which)])]
                        if first:
                            wl += mm_first_waits([4 + g])
                        slot = st["invz_slot"][(ppar, which)]
                        em.op("tensor", lambda t, pp=ppar, w=which, gg=g,
                              f=first, l=last, sl=slot: w_group(
                                  t, pp, w, gg, f, l, sl),
                              waits=wl, inc="pe")
                        st[("slot_free", slot)] = em.v("pe")
                    if jj == NKT - 1:
                        st[("e_war", ppar)] = em.v("pe")

            for which in (0, 1):
                ebuf = eS[(par, which)]
                edone = st[("e_done", par, which)]
                slot = (2 * pi + which) % 8
                em.op("vector", lambda v, e=ebuf: v.tensor_add(
                    zscr[:, 0:4096], e[:, 0:4096], e[:, 4096:8192]),
                    waits=[("act", edone), ("dve", st.get("zscr_free", 0))],
                    inc="dve")
                for n in (2048, 1024, 512):
                    pred = em.v("dve")
                    em.op("vector", lambda v, nn=n: v.tensor_add(
                        zscr[:, 0:nn], zscr[:, 0:nn], zscr[:, nn:2 * nn]),
                        waits=[("dve", pred)], inc="dve")
                pred = em.v("dve")
                em.op("vector", lambda v, sl=slot: v.tensor_reduce(
                    zc_f[:, sl:sl + 1], zscr[:, 0:512], AX.X, ALU.add),
                    waits=[("dve", pred),
                           ("pe", st.get(("slot_free", slot), 0))], inc="dve")
                st["zscr_free"] = em.v("dve")
                pred = em.v("dve")
                em.op("vector", lambda v, sl=slot: v.reciprocal(
                    zc_f[:, sl:sl + 1], zc_f[:, sl:sl + 1]),
                    waits=[("dve", pred)], inc="dve")
                pred = em.v("dve")
                em.op("vector", lambda v, sl=slot: v.tensor_copy(
                    invz_bf[:, sl:sl + 1], zc_f[:, sl:sl + 1]),
                    waits=[("dve", pred)], inc="dve")
                st[("invz_ready", par, which)] = vfence()
                st["invz_slot"][(par, which)] = slot
        pi = NPAIR - 1
        par = pi % 2
        for which in (0, 1):
            for g in range(4):
                slot = st["invz_slot"][(par, which)]
                em.op("tensor", lambda t, w=which, gg=g, pp=par, sl=slot: w_group(
                    t, pp, w, gg, False, True, sl),
                    waits=[("dve", st[("invz_ready", par, which)])], inc="pe")
                st[("slot_free", slot)] = em.v("pe")
        st[("e_war", par)] = em.v("pe")
        st["w_psum_done"] = em.v("pe")
        pv_war[0] = pv_war[1] = [("act", st["t0_drained"])]
        pv_war[2] = pv_war[3] = [("act", st["t1_drained"])]

    def phase_c(c):
        if with_attn:
            for g in range(4):
                em.op("vector", lambda v, gg=g: v.tensor_copy(
                    w_sb[:, gg * 512:(gg + 1) * 512], PW[gg]),
                    waits=[("pe", st["w_psum_done"])], inc="dve")
                read_view(4 + g, "dve")
            wsb_ready = vfence()
            for ft in range(NKT):
                vi = ft % 2
                for (j, obase) in ((2 * ft, 0), (2 * ft + 1, 64)):
                    r, g = j // 4, j % 4
                    wl = [("dve", wsb_ready)] + (mm_first_waits([vi])
                                                 if obase == 0 else [])
                    em.op("tensor", lambda t, rr=r, gg=g, ob=obase, v_=vi: t.matmul(
                        pv(v_)[ob:ob + 64, :], ones_b[32 * rr:32 * rr + 1, 0:64],
                        w_sb[32 * rr:32 * rr + 1, gg * 512:(gg + 1) * 512],
                        start=True, stop=True,
                        tile_position=(32 * rr, ob)),
                        waits=wl, inc="pe" if obase == 64 else None)
                bfill = em.v("pe")
                em.op("vector", lambda v, f=ft, v_=vi: v.tensor_tensor(
                    attn_bf[:, f * 512:(f + 1) * 512],
                    vt_bf[:, f * 512:(f + 1) * 512], pv(v_), ALU.mult),
                    waits=[("pe", bfill)], inc="dve")
                read_view(vi, "dve")
            attn_ready = vfence()
        else:
            em.op("vector", lambda v: v.tensor_copy(attn_bf[:], vt_bf[:]),
                  waits=[("dve", st["qkv_ready"])], inc="dve")
            attn_ready = vfence()
        for k in range(NKT):
            w_kblock("wo", k, k % 2, chunk=c)
            wready = st[("wbf_ready", k % 2)]
            for o in range(NKT):
                wl = [wready, ("dve", attn_ready)]
                if k == 0:
                    wl += mm_first_waits([o])
                em.op("tensor", lambda t, kk=k, oo=o: t.matmul(
                    pv(oo), wbf[kk % 2][:, oo * 128:(oo + 1) * 128],
                    attn_bf[:, kk * 512:(kk + 1) * 512],
                    start=(kk == 0), stop=False),
                    waits=wl, inc="pe" if o == NKT - 1 else None)
            st[("wbf_war", k % 2)] = em.v("pe")
        for o in range(NKT):
            for tt in range(4):
                em.op("tensor", lambda t, oo=o, q=tt: t.matmul(
                    pv(oo)[:, q * 128:(q + 1) * 128],
                    xtm[:, q * 1024 + oo * 128: q * 1024 + (oo + 1) * 128],
                    id_f, is_transpose=True, start=False, stop=(q == 3)),
                    inc="pe" if tt == 3 else None)
        st["xtm_free"] = em.v("pe")
        pfill = em.v("pe")
        for o in range(NKT):
            em.op("vector", lambda v, oo=o: v.tensor_scalar(
                x1f[:, oo * 512:(oo + 1) * 512], pv(oo), boc[:, oo:oo + 1],
                None, ALU.add), waits=[("pe", pfill)], inc="dve")
            read_view(o, "dve")
        layernorm(x1f)
        pred = em.v("dve")
        em.op("vector", lambda v: v.tensor_copy(x1_bf[:], x1f[:]),
              waits=[("dve", pred)], inc="dve")
        st["x1_ready"] = vfence()

    def layernorm(xf):
        pred = em.v("dve")
        em.op("vector", lambda v: v.tensor_copy(xpre_bf[:], xf[:]),
              waits=[("dve", pred)], inc="dve")
        pred = em.v("dve")
        em.op("vector", lambda v: v.tensor_tensor(
            sq_bf[:], xpre_bf[:], xpre_bf[:], ALU.mult),
            waits=[("dve", pred)], inc="dve")
        sq_ready = vfence()
        for (vi, src) in ((0, xpre_bf), (1, sq_bf)):
            for k in range(NKT):
                wl = [("dve", sq_ready)] + (mm_first_waits([vi]) if k == 0 else [])
                em.op("tensor", lambda t, v_=vi, kk=k, s=src: t.matmul(
                    pv(v_)[0:1, :], ones_b[:, 0:1],
                    s[:, kk * 512:(kk + 1) * 512],
                    start=(kk == 0), stop=(kk == NKT - 1)),
                    waits=wl, inc="pe" if k == NKT - 1 else None)
        sfill = em.v("pe")
        em.op("vector", lambda v: v.tensor_scalar(
            mean_row[0:1, :], pv(0)[0:1, :], 1.0 / 1024.0, None, ALU.mult),
            waits=[("pe", sfill)], inc="dve")
        read_view(0, "dve")
        pred = em.v("dve")
        em.op("vector", lambda v: v.tensor_scalar(
            pv(2)[0:1, :], pv(1)[0:1, :], 1.0 / 1024.0, None, ALU.mult),
            waits=[("pe", sfill), ("dve", pred)] + mm_first_waits([2]),
            inc="dve")
        read_view(1, "dve")
        pred = em.v("dve")
        em.op("vector", lambda v: v.tensor_tensor(
            rstd_row[0:1, :], mean_row[0:1, :], mean_row[0:1, :], ALU.mult),
            waits=[("dve", pred)], inc="dve")
        pred = em.v("dve")
        em.op("vector", lambda v: v.tensor_tensor(
            pv(3)[0:1, :], pv(2)[0:1, :], rstd_row[0:1, :], ALU.subtract),
            waits=[("dve", pred)] + mm_first_waits([3]), inc="dve")
        var_ready = vfence()
        em.op("scalar", lambda s: s.activation(pv(2)[0:1, :], pv(3)[0:1, :],
                                               AF.Ln, bias=eps_c[0:1, 0:1]),
              waits=[("dve", var_ready)], inc="act")
        read_view(3, "act")
        pred = em.v("act")
        em.op("scalar", lambda s: s.activation(rstd_row[0:1, :], pv(2)[0:1, :],
                                               AF.Exp, scale=nhalf_c[0:1, 0:1]),
              waits=[("act", pred)], inc="act")
        read_view(2, "act")
        rstd_ready = afence()
        mfence = vfence()
        em.op("tensor", lambda t: t.matmul(pv(4), ones_f[0:1, :],
                                           mean_row[0:1, :],
                                           start=True, stop=True),
              waits=[("act", rstd_ready), ("dve", mfence)] + mm_first_waits([4]),
              inc="pe")
        em.op("tensor", lambda t: t.matmul(pv(5), ones_f[0:1, :],
                                           rstd_row[0:1, :],
                                           start=True, stop=True),
              waits=mm_first_waits([5]), inc="pe")
        bfill = em.v("pe")
        for k in range(NKT):
            pred = em.v("dve")
            em.op("vector", lambda v, kk=k: v.tensor_tensor(
                xf[:, kk * 512:(kk + 1) * 512],
                xf[:, kk * 512:(kk + 1) * 512], pv(4), ALU.subtract),
                waits=[("pe", bfill), ("dve", pred)], inc="dve")
            pred = em.v("dve")
            em.op("vector", lambda v, kk=k: v.tensor_tensor(
                xf[:, kk * 512:(kk + 1) * 512],
                xf[:, kk * 512:(kk + 1) * 512], pv(5), ALU.mult),
                waits=[("dve", pred)], inc="dve")
        read_view(4, "dve")
        read_view(5, "dve")

    def phase_d(c):
        x1r = st["x1_ready"]
        for iq in range(4):
            for k in range(NKT):
                w_kblock("w1", k, k % 2, cols=(iq * 1024, (iq + 1) * 1024),
                         chunk=c)
                wready = st[("wbf_ready", k % 2)]
                for tt in range(4):
                    for half in range(2):
                        wl = [wready, ("dve", x1r)]
                        if k == 0:
                            wl += mm_first_waits([2 * tt + half])
                        lastmm = (tt == 3 and half == 1)
                        em.op("tensor", lambda t, kk=k, q=tt, h=half: t.matmul(
                            P[q].ap()[:, h * 512:(h + 1) * 512],
                            x1_bf[:, kk * 512 + q * 128: kk * 512 + (q + 1) * 128],
                            wbf[kk % 2][:, h * 512:(h + 1) * 512],
                            start=(kk == 0), stop=False), waits=wl,
                            inc="pe" if lastmm else None)
                st[("wbf_war", k % 2)] = em.v("pe")
            for tt in range(4):
                for half in range(2):
                    em.op("tensor", lambda t, q=tt, h=half, qq=iq: t.matmul(
                        P[q].ap()[:, h * 512:(h + 1) * 512], ones_b[0:1, 0:128],
                        b1row[0:1, qq * 1024 + h * 512: qq * 1024 + (h + 1) * 512],
                        start=False, stop=True),
                        inc="pe" if half == 1 else None)
                fill = em.v("pe")
                em.op("vector", lambda v, q=tt, qq=iq: v.tensor_scalar(
                    htm[q][:, qq * 1024:(qq + 1) * 1024], P[q].ap(), 0.0, None,
                    ALU.max), waits=[("pe", fill)], inc="dve")
                read_view(2 * tt, "dve")
                read_view(2 * tt + 1, "dve")
        for kt in range(32):
            vi = kt % 2
            for tt in range(4):
                wl = mm_first_waits([vi]) if tt == 0 else []
                em.op("tensor", lambda t, kk=kt, q=tt, v_=vi: t.matmul(
                    pv(v_)[:, q * 128:(q + 1) * 128],
                    htm[q][:, kk * 128:(kk + 1) * 128], id_b,
                    start=True, stop=True),
                    waits=wl, inc="pe" if tt == 3 else None)
            fill = em.v("pe")
            em.op("vector", lambda v, kk=kt, v_=vi: v.tensor_copy(
                ht_bf[:, kk * 512:(kk + 1) * 512], pv(v_)),
                waits=[("pe", fill)], inc="dve")
            read_view(vi, "dve")
        ht_ready = vfence()
        for kt in range(32):
            w_kblock("w2", kt, kt % 2, chunk=c)
            wready = st[("wbf_ready", kt % 2)]
            for o in range(NKT):
                wl = [wready, ("dve", ht_ready)]
                if kt == 0:
                    wl += mm_first_waits([o])
                em.op("tensor", lambda t, kk=kt, oo=o: t.matmul(
                    pv(oo), wbf[kk % 2][:, oo * 128:(oo + 1) * 128],
                    ht_bf[:, kk * 512:(kk + 1) * 512],
                    start=(kk == 0), stop=(kk == 31)),
                    waits=wl, inc="pe" if o == NKT - 1 else None)
            st[("wbf_war", kt % 2)] = em.v("pe")
        f2fill = em.v("pe")
        for o in range(NKT):
            em.op("vector", lambda v, oo=o: v.scalar_tensor_tensor(
                x2f[:, oo * 512:(oo + 1) * 512], pv(oo), b2c[:, oo:oo + 1],
                x1f[:, oo * 512:(oo + 1) * 512], ALU.add, ALU.add),
                waits=[("pe", f2fill)], inc="dve")
            read_view(o, "dve")
        layernorm(x2f)
        x2_ready = vfence()
        for tt in range(4):
            base = (tt % 2) * 2
            for o in range(NKT):
                wl = [("dve", x2_ready),
                      ("dO", st.get(("outtm_free", tt % 2), 0))]
                if o == 0:
                    wl += mm_first_waits([base, base + 1])
                em.op("tensor", lambda t, q=tt, oo=o, b=base: t.matmul(
                    P[b // 2].ap()[:, oo * 128:(oo + 1) * 128],
                    x2f[:, oo * 512 + q * 128: oo * 512 + (q + 1) * 128],
                    id_f, is_transpose=True,
                    start=True, stop=True),
                    waits=wl, inc="pe" if o == NKT - 1 else None)
            fill = em.v("pe")
            em.op("vector", lambda v, q=tt, b=base: v.tensor_copy(
                outtm[q % 2][:], P[b // 2].ap()), waits=[("pe", fill)],
                inc="dve")
            read_view(base, "dve")
            read_view(base + 1, "dve")
            evac = vfence()
            em.op("gpsimd", lambda g, q=tt, cc=c: g.dma_start(
                out=dout[cc * CH + q * 128: cc * CH + (q + 1) * 128, :],
                in_=outtm[q % 2][:]), waits=[("dve", evac)], inc="dO", inc_n=16)
            st[("outtm_free", tt % 2)] = em.v("dO")

    ldconst()
    weight_prepass()
    from concourse.hw_specs import get_activation_tables
    set_id = None
    for idx, (nm, funcs) in enumerate(get_activation_tables(nc.m.arch).items()):
        if AF.Exp in funcs and AF.Ln in funcs and AF.Copy in funcs:
            set_id = idx
            break
    if set_id is not None:
        em.op("scalar", lambda s, i=set_id: s.add_instruction(
            mybir.InstLoadActFuncSet(
                name=s.bass.get_next_instruction_name(),
                act_func_set_id=i, ins=[], outs=[])))

    for c in range(n_chunks):
        phase_a(c)
        if with_attn:
            phase_b(c)
        phase_c(c)
        phase_d(c)

    if debug_dump:
        dk_early = nc.dram_tensor("dbg_kt_early", [128, 4096], BF16,
                                  kind="ExternalOutput").ap()
        em.op("gpsimd", lambda g: g.dma_start(out=dk_early, in_=kt_bf),
              waits=[("dve", st["qkv_ready"])], inc="dO", inc_n=16)
        dbg = {"kt_bf": kt_bf, "vt_bf": vt_bf, "attn_bf": attn_bf,
               "qs_bf": qs_bf, "w_sb": w_sb, "x1f": x1f, "x1_bf": x1_bf,
               "x2f": x2f, "zc_f": zc_f, "invz_bf": invz_bf,
               "ht_bf": ht_bf, "e10": eS[(1, 0)], "e11": eS[(1, 1)],
               "htm0": htm[0], "sq_bf": sq_bf, "xpre_bf": xpre_bf}
        allwait = [("dve", em.v("dve")), ("act", em.v("act")),
                   ("pe", em.v("pe"))]
        for nm, ap in dbg.items():
            dt = ap.dtype
            dd = nc.dram_tensor("dbg_" + nm, list(ap.shape), dt,
                                kind="ExternalOutput").ap()
            em.op("gpsimd", lambda g, a=ap, d=dd: g.dma_start(out=d, in_=a),
                  waits=allwait, inc="dO", inc_n=16)

    total_out = em.v("dO")
    print("instruction counts:", em.n_ops())

    stack = contextlib.ExitStack()
    sems = {s: stack.enter_context(nc.semaphore(s))
            for s in ("dW", "dX", "dO", "dS", "pe", "act", "dve")}
    block = stack.enter_context(nc.Block())

    @block.sync
    def _(sync):
        em.play("sync", sync, sems)

    @block.tensor
    def _(tensor):
        em.play("tensor", tensor, sems)

    @block.scalar
    def _(scalar):
        em.play("scalar", scalar, sems)

    @block.vector
    def _(vector):
        em.play("vector", vector, sems)

    @block.gpsimd
    def _(gpsimd):
        em.play("gpsimd", gpsimd, sems)
        gpsimd.wait_ge(sems["dO"], total_out)

    stack.close()
    return nc


def host_inputs(x_shard, args):
    import ml_dtypes

    def col(b):
        return np.ascontiguousarray(np.asarray(b, dtype=np.float32).reshape(8, 128).T)

    return {
        "x": np.ascontiguousarray(x_shard, dtype=np.float32),
        "wq": np.asarray(args["L0_attn_Wq"], dtype=np.float32),
        "wk": np.asarray(args["L0_attn_Wk"], dtype=np.float32),
        "wv": np.asarray(args["L0_attn_Wv"], dtype=np.float32),
        "wo": np.asarray(args["L0_attn_Wo"], dtype=np.float32),
        "w1": np.asarray(args["L0_ffn_W1"], dtype=np.float32),
        "w2": np.asarray(args["L0_ffn_W2"], dtype=np.float32),
        "bq": col(args["L0_attn_bq"]), "bk": col(args["L0_attn_bk"]),
        "bv": col(args["L0_attn_bv"]), "bo": col(args["L0_attn_bo"]),
        "b2": col(args["L0_ffn_b2"]),
        "b1": np.ascontiguousarray(
            np.asarray(args["L0_ffn_b1"], dtype=np.float32).reshape(1, 4096)),
        "id_f": np.eye(128, dtype=np.float32),
        "id_b": np.eye(128).astype(ml_dtypes.bfloat16),
        "ones_b": np.ones((128, 128), dtype=ml_dtypes.bfloat16),
        "ones_f": np.ones((128, 128), dtype=np.float32),
    }


_NC_CACHE = {}


def _get_nc():
    if "nc" not in _NC_CACHE:
        _NC_CACHE["nc"] = build_program(n_chunks=4)
    return _NC_CACHE["nc"]


def kernel(**inputs):
    x = np.asarray(inputs["L0_attn_X_embed"], dtype=np.float32)
    N, L, E = x.shape
    assert (N, L, E) == (2, 8192, 1024)
    flat = np.ascontiguousarray(x.reshape(N * L, E))
    shards = [flat[c * 2048:(c + 1) * 2048] for c in range(8)]
    nc = _get_nc()
    base = host_inputs(shards[0], inputs)
    in_maps = [base]
    for c in range(1, 8):
        m = dict(base)
        m["x"] = np.ascontiguousarray(shards[c])
        in_maps.append(m)
    res = run_bass_kernel_spmd(nc, in_maps, core_ids=list(range(8)))
    out = np.concatenate([r["out"] for r in res.results], axis=0)
    return out.reshape(N, L, E)
